# version 02 (final)
# baseline (speedup 1.0000x reference)
import math
from contextlib import ExitStack

import numpy as np
import ml_dtypes

import concourse.bass as bass
import concourse.mybir as mybir
from concourse.bass_utils import run_bass_kernel_spmd

F32 = mybir.dt.float32
BF16 = mybir.dt.bfloat16
AF = mybir.ActivationFunctionType
ALU = mybir.AluOpType
AX = mybir.AxisListType

T_CORE = 2048
NTT = 16
D = 1024
NDC = 8
LN_EPS = 1e-5
Z_EPS = float(np.finfo(np.float32).eps)
INV_SQRT_DH = 1.0 / math.sqrt(128.0)

SEMS = ("sld", "gld", "gst", "te", "ve", "se", "ge", "cc",
        "dma0", "dma1", "dma2", "dma3", "dma4", "dma5", "dma6", "dma7",
        "dma8", "dma9", "dma10", "dma11", "dma12")


class Plan:
    ENGS = ("sync", "gpsimd", "tensor", "vector", "scalar")

    def __init__(self):
        self.recs = {e: [] for e in self.ENGS}
        self.counts = {s: 0 for s in SEMS}
        self.bankfree = {}

    def inc(self, sem, n):
        self.counts[sem] += n
        return self.counts[sem]

    def val(self, sem):
        return self.counts[sem]

    def op(self, eng, fn, waits=(), inc=None, inc_n=1):
        waits = [w for w in waits if w is not None]
        for (s, v) in waits:
            assert v <= self.val(s), f"wait {s}>={v} exceeds planned {self.val(s)}"
        tgt = self.inc(inc, inc_n) if inc else None

        def run(ctx, fn=fn, waits=tuple(waits), inc=inc, inc_n=inc_n):
            en = ctx["eng"]
            for (s, v) in waits:
                en.wait_ge(ctx["sem"][s], v)
            instr = fn(ctx)
            if inc is not None:
                instr.then_inc(ctx["sem"][inc], inc_n)
        self.recs[eng].append(run)
        return tgt

    def fence(self):
        return tuple((s, self.counts[s]) for s in SEMS if self.counts[s] > 0)

    def use_bank(self, bk):
        return self.bankfree.get(bk)

    def release_bank(self, bk, sem, val):
        self.bankfree[bk] = (sem, val)


def _bf16(a):
    return np.asarray(a, dtype=ml_dtypes.bfloat16)


_CONST = {}


def host_constants():
    if not _CONST:
        n = D
        j = np.arange(n, dtype=np.float64)
        th = 2.0 * np.pi * np.outer(j, j) / n
        _CONST.update(
            COS=_bf16(np.cos(th)),
            SIN=_bf16(np.sin(th)),
            GR=_bf16(np.cos(th) / n),
            GSIN=_bf16(np.sin(th) / n),
            GINEG=_bf16(-np.sin(th) / n),
            IDEN=np.eye(128, dtype=np.float32),
        )
    return _CONST


def finish(P, sem, block, top, nc):
    def replay(name):
        def f(engine):
            ctx = {"eng": engine, "sem": sem}
            if name == "gpsimd":
                ctx["pid"] = engine.partition_id()
            for r in P.recs[name]:
                r(ctx)
        return f

    block.sync(replay("sync"))
    block.gpsimd(replay("gpsimd"))
    block.tensor(replay("tensor"))
    block.vector(replay("vector"))
    block.scalar(replay("scalar"))
    top.close()
    return nc


def build_nc(debug=None):
    nc = bass.Bass("TRN2", target_bir_lowering=False)
    P = Plan()

    dram = {}

    def din(name, shape, dtype=F32):
        dram[name] = nc.dram_tensor(name, list(shape), dtype, kind="ExternalInput").ap()
        return dram[name]

    X_in = din("x", (T_CORE, D))
    WQKVT = din("wqkvT", (D, 3 * D))
    WPROJT = din("wprojT", (D, D))
    W1 = din("w1", (D, 4 * D))
    W2 = din("w2", (4 * D, D))
    WFTR = din("wfTr", (D, D))
    WFTI = din("wfTi", (D, D))
    CCOS = din("c_cos", (D, D), BF16)
    CSIN = din("c_sin", (D, D), BF16)
    CGR = din("c_gr", (D, D), BF16)
    CGSIN = din("c_gsin", (D, D), BF16)
    CGINEG = din("c_gineg", (D, D), BF16)
    IDEN = din("iden", (128, 128))
    BPROJ = din("bproj", (128, 8))
    B1 = din("b1", (128, 32))
    B2 = din("b2", (128, 8))
    L1B = din("l1b", (128, 8))
    VQT = din("vqT", (128, 32))
    VKT = din("vkT", (128, 32))
    VVT = din("vvT", (128, 32))
    UEQ = din("ueq", (33, 128))
    UEK = din("uek", (33, 128))
    UEV = din("uev", (33, 128))
    SSEL = din("ssel", (16, 8, 128))

    OUT = nc.dram_tensor("y", [T_CORE, D], F32, kind="ExternalOutput").ap()

    M_DRAM = nc.dram_tensor("m_dram", [128, NDC, D], BF16, kind="Internal").ap()
    AR_IN = nc.dram_tensor("ar_in", [128, 1032], F32, kind="Internal").ap()
    AR_OUT = nc.dram_tensor("ar_out", [128, 1032], F32, kind="Internal").ap()
    CC1_IN = nc.dram_tensor("cc1_in", [128, NDC, T_CORE], BF16, kind="Internal").ap()
    CC1_OUT = nc.dram_tensor("cc1_out", [8, 128, NDC, T_CORE], BF16, kind="Internal").ap()
    CC2_IN = nc.dram_tensor("cc2_in", [128, 4, 4096], BF16, kind="Internal").ap()
    CC2_OUT = nc.dram_tensor("cc2_out", [8, 128, 4 * 4096], BF16, kind="Internal").ap()

    dbg = {}
    if debug == "M":
        dbg["M"] = nc.dram_tensor("dbg_M", [128, NDC, D], F32, kind="ExternalOutput").ap()
    if debug in ("B1", "LN1"):
        dbg["xln"] = nc.dram_tensor("dbg_xln", [128, NDC, T_CORE], BF16, kind="ExternalOutput").ap()
        dbg["stats"] = nc.dram_tensor("dbg_stats", [128, 8 * NTT], F32, kind="ExternalOutput").ap()
    if debug == "B1":
        dbg["q"] = nc.dram_tensor("dbg_q", [128, NDC, T_CORE], BF16, kind="ExternalOutput").ap()
        dbg["ct"] = nc.dram_tensor("dbg_ct", [128, 1032], F32, kind="ExternalOutput").ap()
        dbg["z"] = nc.dram_tensor("dbg_z", [16, T_CORE], F32, kind="ExternalOutput").ap()
        dbg["attn"] = nc.dram_tensor("dbg_attn", [128, NDC, T_CORE], BF16, kind="ExternalOutput").ap()
        dbg["ctp"] = nc.dram_tensor("dbg_ctp", [128, 1032], F32, kind="ExternalOutput").ap()
    if debug in ("B2", "B2LN"):
        dbg["xln2"] = nc.dram_tensor("dbg_xln2", [128, NDC, T_CORE], BF16, kind="ExternalOutput").ap()
    if debug == "B2":
        dbg["h"] = nc.dram_tensor("dbg_h", [128, 32, D], BF16, kind="ExternalOutput").ap()

    top = ExitStack()
    et = top.enter_context

    res = et(nc.sbuf_tensor("res", [128, NTT, D], F32))
    iden = et(nc.sbuf_tensor("iden_sb", [128, 128], F32))
    ones_col = et(nc.sbuf_tensor("ones_col", [128, 1], BF16))
    bias_all = et(nc.sbuf_tensor("bias_all", [128, 56], F32))
    stats_f = et(nc.sbuf_tensor("stats", [128, 8 * NTT], F32))

    def SC(cq):
        return stats_f[:, cq * NTT:(cq + 1) * NTT]

    def SC1(cq, tt):
        return stats_f[:, cq * NTT + tt:cq * NTT + tt + 1]

    pq = [et(nc.psum_tensor(f"pq{i}", [128, 1024], F32)) for i in range(4)]

    def bank(i):
        return pq[i // 2][:, (i % 2) * 512:(i % 2) * 512 + 512]

    sem = {s: et(nc.semaphore(s)) for s in SEMS}
    block = et(nc.Block())

    for dst, src in ((iden[:], IDEN), (bias_all[:, 0:8], BPROJ), (bias_all[:, 8:40], B1),
                     (bias_all[:, 40:48], B2), (bias_all[:, 48:56], L1B)):
        P.op("sync", lambda c, dst=dst, src=src: c["eng"].dma_start(out=dst, in_=src),
             inc="dma0", inc_n=16)
    P.op("vector", lambda c: c["eng"].memset(ones_col[:], 1.0), inc="ve")
    x_r = X_in.rearrange("(t p) d -> p t d", p=128)
    for g in range(4):
        P.op("sync", lambda c, g=g: c["eng"].dma_start(
            out=res[:, 4 * g:4 * g + 4, :], in_=x_r[:, 4 * g:4 * g + 4, :]),
            inc="dma1", inc_n=16)
    x_loaded = ("dma1", P.val("dma1"))
    smalls_loaded = ("dma0", P.val("dma0"))

    ph0 = ExitStack()
    e0 = ph0.enter_context
    wftr = e0(nc.sbuf_tensor("wftr", [128, NDC, D], BF16))
    wfti = e0(nc.sbuf_tensor("wfti", [128, NDC, D], BF16))
    grb = e0(nc.sbuf_tensor("grb", [128, NDC, D], BF16))
    gxb = e0(nc.sbuf_tensor("gxb", [128, NDC, D], BF16))
    prb = e0(nc.sbuf_tensor("prb", [128, NDC, D], BF16))
    pib = e0(nc.sbuf_tensor("pib", [128, NDC, D], BF16))
    cbuf = e0(nc.sbuf_tensor("cbuf", [128, 2, 2, NDC, 128], BF16))
    mrow = e0(nc.sbuf_tensor("mrow", [128, D], BF16))

    for dst, src in ((wftr, WFTR), (wfti, WFTI)):
        P.op("gpsimd", lambda c, dst=dst, src=src: c["eng"].dma_start(
            out=dst[:], in_=src.rearrange("(oc p) d -> p oc d", p=128)),
            inc="dma2", inc_n=16)
    wft_done = ("dma2", P.val("dma2"))
    P.op("sync", lambda c: c["eng"].dma_start(
        out=grb[:], in_=CGR.rearrange("(oc p) m -> p oc m", p=128)), inc="dma3", inc_n=16)
    P.op("sync", lambda c: c["eng"].dma_start(
        out=gxb[:], in_=CGINEG.rearrange("(oc p) m -> p oc m", p=128)), inc="dma3", inc_n=16)
    g_loaded = ("dma3", P.val("dma3"))

    def emit_P(outbuf, gx_pairs, base_waits):
        done = []
        for dc in range(8):
            for nb in range(2):
                bk = (dc % 2) * 2 + nb
                waits = list(base_waits)
                bw = P.use_bank(bk)
                if bw:
                    waits.append(bw)
                first = True
                n_mm = 8 * len(gx_pairs)
                i_mm = 0
                for oc in range(8):
                    for (wbuf, gbuf_) in gx_pairs:
                        i_mm += 1
                        last = i_mm == n_mm
                        P.op("tensor", lambda c, bk=bk, wbuf=wbuf, gbuf_=gbuf_, oc=oc, dc=dc, nb=nb, first=first, last=last:
                             c["eng"].matmul(bank(bk), wbuf[:, oc, dc * 128:dc * 128 + 128],
                                             gbuf_[:, oc, nb * 512:nb * 512 + 512],
                                             start=first, stop=last),
                             waits=waits if first else (), inc="te" if last else None)
                        first = False
                t = P.val("te")
                v = P.op("scalar", lambda c, bk=bk, dc=dc, nb=nb, outbuf=outbuf:
                         c["eng"].activation(outbuf[:, dc, nb * 512:nb * 512 + 512],
                                             bank(bk), AF.Identity),
                         waits=[("te", t)], inc="se")
                P.release_bank(bk, "se", v)
                done.append(v)
        return done

    emit_P(prb, [(wftr, grb), (wfti, gxb)],
           [wft_done, g_loaded])
    pr_done = P.val("se")
    te_after_pr = P.val("te")
    P.op("sync", lambda c: c["eng"].dma_start(
        out=gxb[:], in_=CGSIN.rearrange("(oc p) m -> p oc m", p=128)),
        waits=[("te", te_after_pr)], inc="dma3", inc_n=16)
    gsin_loaded = ("dma3", P.val("dma3"))
    emit_P(pib, [(wftr, gxb), (wfti, grb)],
           [gsin_loaded])
    pi_done = P.val("se")

    c_tgt = {}
    c_te = {}

    def load_cbuf(jc, waits):
        b = jc % 2
        sm = "dma4" if b == 0 else "dma12"
        for k, src in enumerate((CCOS, CSIN)):
            P.op("sync", lambda c, b=b, k=k, src=src, jc=jc: c["eng"].dma_start(
                out=cbuf[:, b, k, :, :],
                in_=src[:, jc * 128:jc * 128 + 128].rearrange("(dc p) j -> p dc j", p=128)),
                waits=waits if k == 0 else (), inc=sm, inc_n=16)
        c_tgt[jc] = (sm, P.val(sm))

    load_cbuf(0, [])
    load_cbuf(1, [])

    mrow_free = {0: None, 1: None}
    for jc in range(8):
        te_last = None
        for nb in range(2):
            bk = 4 + (jc % 2) * 2 + nb
            waits = [c_tgt[jc], ("se", pi_done)]
            bw = P.use_bank(bk)
            if bw:
                waits.append(bw)
            first = True
            for dc in range(8):
                for comp, pbufx in ((0, prb), (1, pib)):
                    last = (dc == 7) and (comp == 1)
                    P.op("tensor", lambda c, bk=bk, jc=jc, dc=dc, comp=comp, pbufx=pbufx, nb=nb, first=first, last=last:
                         c["eng"].matmul(bank(bk), cbuf[:, jc % 2, comp, dc, :],
                                         pbufx[:, dc, nb * 512:nb * 512 + 512],
                                         start=first, stop=last),
                         waits=waits if first else (), inc="te" if last else None)
                    first = False
            t = P.val("te")
            te_last = t
            waits = [("te", t)]
            if mrow_free[nb]:
                waits.append(mrow_free[nb])
            v = P.op("vector", lambda c, bk=bk, nb=nb: c["eng"].tensor_copy(
                mrow[:, nb * 512:nb * 512 + 512], bank(bk)), waits=waits, inc="ve")
            P.release_bank(bk, "ve", v)
            sm = "dma5" if nb == 0 else "dma6"
            s = P.op("sync", lambda c, jc=jc, nb=nb: c["eng"].dma_start(
                out=M_DRAM[:, jc, nb * 512:nb * 512 + 512],
                in_=mrow[:, nb * 512:nb * 512 + 512]),
                waits=[("ve", v)], inc=sm, inc_n=16)
            mrow_free[nb] = (sm, s)
        c_te[jc] = te_last
        if jc + 2 < 8:
            load_cbuf(jc + 2, [("te", te_last)])
    m_stored = [("dma5", P.val("dma5")), ("dma6", P.val("dma6"))]

    ph0_fence = P.fence()
    ph0.close()

    xln = et(nc.sbuf_tensor("xln", [128, NDC, T_CORE], BF16))
    stage = et(nc.sbuf_tensor("stage", [128, NDC, 512], F32))
    lnst = [et(nc.sbuf_tensor(f"lnst{i}", [128, D], F32)) for i in range(2)]
    outtm = [et(nc.sbuf_tensor(f"outtm{i}", [128, D], F32)) for i in range(2)]
    dump = et(nc.sbuf_tensor("dump", [128, D], BF16))

    lnst_free = {0: None, 1: None}
    outtm_free = {0: None, 1: None}
    stats_state = {"sum_ready": None}

    def emit_ln_to_xln(fence_waits=()):
        sum_ready = stats_state["sum_ready"]
        first = stats_state.get("first", True)
        stats_state["first"] = False
        mz = P.op("vector", lambda c: c["eng"].memset(SC(1), 0.0),
                  waits=list(fence_waits), inc="ve")
        gp_done = [mz]
        for tt in range(NTT):
            w = list(fence_waits)
            gate = [x_loaded, smalls_loaded] if first else [sum_ready]
            P.op("vector", lambda c, tt=tt: c["eng"].reduce_sum(
                SC1(0, tt), res[:, tt, :], axis=AX.X),
                waits=w + gate, inc="ve")
            g = P.op("vector", lambda c, tt=tt: c["eng"].scalar_tensor_tensor(
                dump[:], res[:, tt, :], 1.0, res[:, tt, :], ALU.mult, ALU.mult,
                accum_out=SC1(1, tt)),
                waits=w + [("ve", mz)] + gate,
                inc="ve")
            gp_done.append(g)
        wv = [("ve", gp_done[-1])] + list(fence_waits)
        mu = P.op("vector", lambda c: c["eng"].tensor_scalar_mul(
            SC(2), SC(0), 1.0 / D), waits=wv, inc="ve")
        sq = P.op("scalar", lambda c: c["eng"].square(SC(3), SC(2)),
                  waits=[("ve", mu)], inc="se")
        va = P.op("vector", lambda c: c["eng"].scalar_tensor_tensor(
            SC(7), SC(1), 1.0 / D, SC(3),
            ALU.mult, ALU.subtract), waits=[("se", sq)], inc="ve")
        vp = P.op("vector", lambda c: c["eng"].tensor_scalar_add(
            SC(6), SC(7), LN_EPS), waits=[("ve", va)], inc="ve")
        rv = P.op("vector", lambda c: c["eng"].reciprocal(
            SC(5), SC(6)), waits=[("ve", vp)], inc="ve")
        rs = P.op("scalar", lambda c: c["eng"].activation(
            SC(4), SC(5), AF.Sqrt), waits=[("ve", rv)], inc="se")
        tp_done = {}
        for tt in range(NTT):
            b = tt % 2
            w = [("se", rs)] + list(fence_waits)
            if lnst_free[b] is not None:
                w.append(lnst_free[b])
            av = P.op("vector", lambda c, tt=tt, b=b: c["eng"].tensor_scalar(
                lnst[b][:], res[:, tt, :], SC1(2, tt), SC1(4, tt),
                ALU.subtract, ALU.mult), waits=w, inc="ve")
            for half in range(2):
                bk = b * 2 + half
                bw = P.use_bank(bk)
                for q in range(4):
                    dc = half * 4 + q
                    P.op("tensor", lambda c, bk=bk, dc=dc, q=q, b=b: c["eng"].transpose(
                        bank(bk)[:, q * 128:q * 128 + 128],
                        lnst[b][:, dc * 128:dc * 128 + 128], iden[:]),
                        waits=([("ve", av)] + list(fence_waits)
                               + ([bw] if bw else [])) if q == 0 else (),
                        inc="te" if q == 3 else None)
                t = P.val("te")
                eng = "scalar" if half == 0 else "vector"
                sv = P.op(eng, lambda c, bk=bk, half=half, tt=tt: (
                    c["eng"].activation(xln[:, half * 4:half * 4 + 4, tt * 128:tt * 128 + 128],
                                        bank(bk).rearrange("p (q n) -> p q n", q=4), AF.Identity)
                    if half == 0 else
                    c["eng"].tensor_copy(xln[:, half * 4:half * 4 + 4, tt * 128:tt * 128 + 128],
                                         bank(bk).rearrange("p (q n) -> p q n", q=4))),
                    waits=[("te", t)], inc="se" if half == 0 else "ve")
                P.release_bank(bk, "se" if half == 0 else "ve", sv)
                tp_done[(tt, half)] = (("se" if half == 0 else "ve"), sv)
            lnst_free[b] = ("te", P.val("te"))
        return tp_done[(NTT - 1, 0)], tp_done[(NTT - 1, 1)]

    def emit_epilogue_tile(tt, src_fn, src_waits, ident=None):
        b = tt % 2
        tm = pq[2 + b]
        bw1 = P.use_bank(4 + 2 * b)
        bw2 = P.use_bank(5 + 2 * b)
        w0 = list(src_waits) + [x for x in (bw1, bw2) if x]
        idn = iden if ident is None else ident
        for oc in range(8):
            P.op("tensor", lambda c, oc=oc, tm=tm, src_fn=src_fn, idn=idn: c["eng"].transpose(
                tm[:, oc * 128:oc * 128 + 128], src_fn(oc), idn[:]),
                waits=w0 if oc == 0 else (), inc="te" if oc == 7 else None)
        t = P.val("te")
        w = [("te", t)]
        if outtm_free[b] is not None:
            w.append(outtm_free[b])
        P.op("scalar", lambda c, tt=tt: c["eng"].memzero(SC1(1, tt)))
        ev = P.op("scalar", lambda c, tm=tm, b=b, tt=tt: c["eng"].activation(
            outtm[b][:], tm[:], AF.Identity, accum_out=SC1(1, tt)),
            waits=w, inc="se")
        P.release_bank(4 + 2 * b, "se", ev)
        P.release_bank(5 + 2 * b, "se", ev)
        mz6 = P.op("vector", lambda c, tt=tt: c["eng"].memset(SC1(6, tt), 0.0),
                   inc="ve")
        g2 = P.op("vector", lambda c, b=b, tt=tt: c["eng"].scalar_tensor_tensor(
            dump[:], outtm[b][:], 1.0, outtm[b][:], ALU.mult, ALU.mult,
            accum_out=SC1(6, tt)), waits=[("se", ev), ("ve", mz6)], inc="ve")
        mu = P.op("vector", lambda c, tt=tt: c["eng"].tensor_scalar_mul(
            SC1(2, tt), SC1(1, tt), 1.0 / D), waits=[("se", ev)], inc="ve")
        sq = P.op("scalar", lambda c, tt=tt: c["eng"].square(
            SC1(3, tt), SC1(2, tt)), waits=[("ve", mu)], inc="se")
        va = P.op("vector", lambda c, tt=tt: c["eng"].scalar_tensor_tensor(
            SC1(7, tt), SC1(6, tt), 1.0 / D, SC1(3, tt),
            ALU.mult, ALU.subtract), waits=[("se", sq), ("ve", g2)], inc="ve")
        vp = P.op("vector", lambda c, tt=tt: c["eng"].tensor_scalar_add(
            SC1(1, tt), SC1(7, tt), LN_EPS), waits=[("ve", va)], inc="ve")
        rv = P.op("vector", lambda c, tt=tt: c["eng"].reciprocal(
            SC1(5, tt), SC1(1, tt)), waits=[("ve", vp)], inc="ve")
        rs = P.op("scalar", lambda c, tt=tt: c["eng"].activation(
            SC1(4, tt), SC1(5, tt), AF.Sqrt), waits=[("ve", rv)], inc="se")
        wl = [("se", rs)]
        if lnst_free[b] is not None:
            wl.append(lnst_free[b])
        lv = P.op("vector", lambda c, tt=tt, b=b: c["eng"].tensor_scalar(
            lnst[b][:], outtm[b][:], SC1(2, tt), SC1(4, tt),
            ALU.subtract, ALU.mult), waits=wl, inc="ve")
        outtm_free[b] = ("ve", lv)
        mz0 = P.op("vector", lambda c, tt=tt: c["eng"].memset(SC1(0, tt), 0.0),
                   inc="ve")
        rv2 = P.op("vector", lambda c, tt=tt, b=b: c["eng"].scalar_tensor_tensor(
            res[:, tt, :], lnst[b][:], 1.0, res[:, tt, :], ALU.mult, ALU.add,
            accum_out=SC1(0, tt)), waits=[("ve", mz0), ("ve", lv)], inc="ve")
        lnst_free[b] = ("ve", rv2)
        return ("ve", rv2)

    ph1 = ExitStack()
    e1 = ph1.enter_context
    q_fm = e1(nc.sbuf_tensor("q_fm", [128, NDC, T_CORE], BF16))
    wstr = [e1(nc.sbuf_tensor(f"wstr{i}", [128, NDC, 128], BF16)) for i in range(2)]
    wkh = e1(nc.sbuf_tensor("wkh", [128, NDC, 512], BF16))
    wvh = e1(nc.sbuf_tensor("wvh", [128, NDC, 512], BF16))
    ktile = [e1(nc.sbuf_tensor(f"ktile{i}", [128, 512], BF16)) for i in range(2)]
    vtile = [e1(nc.sbuf_tensor(f"vtile{i}", [128, 512], BF16)) for i in range(2)]
    ct_bf = e1(nc.sbuf_tensor("ct_bf", [128, NDC, 128], BF16))
    kdiag = e1(nc.sbuf_tensor("kdiag", [128, 8, 16], BF16))
    z_sb = e1(nc.sbuf_tensor("z_sb", [16, T_CORE], F32))
    zbc_sb = [e1(nc.sbuf_tensor("zbc0", [128, 512], F32))]
    ssel_sb = e1(nc.sbuf_tensor("ssel_sb", [16, 8, 128], F32))
    ct_f32 = e1(nc.sbuf_tensor("ct_f32", [128, 1032], F32))

    P.op("sync", lambda c: c["eng"].dma_start(out=ssel_sb[:], in_=SSEL),
         waits=list(ph0_fence), inc="dma0", inc_n=16)
    ssel_loaded = ("dma0", P.val("dma0"))

    emit_ln_to_xln(fence_waits=list(ph0_fence))
    xln_ready = [("se", P.val("se")), ("ve", P.val("ve"))]
    dbg_waits = []
    if debug in ("B1", "LN1"):
        P.op("sync", lambda c: c["eng"].dma_start(out=dbg["xln"], in_=xln[:]),
             waits=xln_ready, inc="gst", inc_n=16)
        P.op("sync", lambda c: c["eng"].dma_start(out=dbg["stats"], in_=stats_f[:]),
             waits=xln_ready, inc="gst", inc_n=16)
        dbg_waits = [("gst", P.val("gst"))]
    if debug == "LN1":
        fin = P.val("gst")
        P.op("gpsimd", lambda c, fin=fin: c["eng"].engine_nop(), waits=[("gst", fin)])
        ph1.close()
        return finish(P, sem, block, top, nc)

    wq_tgt = {}
    wstr_free = {0: None, 1: None}

    def load_wstream(key, src_ap, waits, b):
        if wstr_free[b] is not None:
            waits = list(waits) + [wstr_free[b]]
        sm = "dma7" if b == 0 else "dma8"
        P.op("gpsimd", lambda c, b=b, src_ap=src_ap: c["eng"].dma_start(
            out=wstr[b][:], in_=src_ap), waits=waits, inc=sm, inc_n=16)
        wq_tgt[key] = (sm, P.val(sm))

    def load_wq(oc, waits):
        load_wstream(oc, WQKVT[:, oc * 128:oc * 128 + 128].rearrange(
            "(dc p) o -> p dc o", p=128), waits, oc % 2)

    load_wq(0, list(ph0_fence))
    load_wq(1, [])
    q_te = {}
    for oc in range(8):
        for dc in range(8):
            for tg in range(4):
                bk = tg
                bw = P.use_bank(bk) if dc == 0 else None
                P.op("tensor", lambda c, oc=oc, dc=dc, tg=tg, bk=bk: c["eng"].matmul(
                    bank(bk), wstr[oc % 2][:, dc, :], xln[:, dc, tg * 512:tg * 512 + 512],
                    start=(dc == 0), stop=(dc == 7)),
                    waits=([wq_tgt[oc]] + xln_ready + ([bw] if bw else []))
                    if dc == 0 else (),
                    inc="te" if dc == 7 else None)
        q_te[oc] = P.val("te")
        wstr_free[oc % 2] = ("te", q_te[oc])
        for tg in range(4):
            bk = tg
            tstop = q_te[oc] - (3 - tg)
            sv = P.op("scalar", lambda c, oc=oc, tg=tg, bk=bk: c["eng"].activation(
                q_fm[:, oc, tg * 512:tg * 512 + 512], bank(bk), AF.Relu),
                waits=[("te", tstop)], inc="se")
            P.release_bank(bk, "se", sv)
        if oc + 2 < 8:
            load_wq(oc + 2, [])

    kv_tgt = {}

    def load_wkv(half, waits):
        P.op("gpsimd", lambda c, half=half: c["eng"].dma_start(
            out=wkh[:],
            in_=WQKVT[:, D + half * 512:D + half * 512 + 512].rearrange(
                "(dc p) o -> p dc o", p=128)), waits=waits, inc="dma9", inc_n=16)
        P.op("gpsimd", lambda c, half=half: c["eng"].dma_start(
            out=wvh[:],
            in_=WQKVT[:, 2 * D + half * 512:2 * D + half * 512 + 512].rearrange(
                "(dc p) o -> p dc o", p=128)), inc="dma9", inc_n=16)
        kv_tgt[half] = ("dma9", P.val("dma9"))

    load_wkv(0, list(ph0_fence))
    ktile_free = {0: None, 1: None}
    vtile_free = {0: None, 1: None}
    ct_evac = []
    for half in range(2):
        for tt in range(NTT):
            b = tt % 2
            bk_k, bk_v = b, 2 + b
            bwk = P.use_bank(bk_k)
            bwv = P.use_bank(bk_v)
            w0 = [kv_tgt[half]] + xln_ready + [x for x in (bwk, bwv) if x]
            for dc in range(8):
                P.op("tensor", lambda c, dc=dc, tt=tt, bk_k=bk_k: c["eng"].matmul(
                    bank(bk_k), xln[:, dc, tt * 128:tt * 128 + 128], wkh[:, dc, :],
                    start=(dc == 0), stop=(dc == 7)),
                    waits=w0 if dc == 0 else (), inc=None)
                P.op("tensor", lambda c, dc=dc, tt=tt, bk_v=bk_v: c["eng"].matmul(
                    bank(bk_v), xln[:, dc, tt * 128:tt * 128 + 128], wvh[:, dc, :],
                    start=(dc == 0), stop=(dc == 7)),
                    inc="te" if dc == 7 else None)
            t = P.val("te")
            wk = [("te", t)] + ([ktile_free[b]] if ktile_free[b] else [])
            kv_ = P.op("scalar", lambda c, b=b, bk_k=bk_k: c["eng"].activation(
                ktile[b][:], bank(bk_k), AF.Relu), waits=wk, inc="se")
            P.release_bank(bk_k, "se", kv_)
            wv_ = [("te", t)] + ([vtile_free[b]] if vtile_free[b] else [])
            vv_ = P.op("vector", lambda c, b=b, bk_v=bk_v: c["eng"].tensor_copy(
                vtile[b][:], bank(bk_v)), waits=wv_, inc="ve")
            P.release_bank(bk_v, "ve", vv_)
            for j in range(4):
                bwait = P.use_bank(4) if (tt == 0 and j == 0) else None
                bwait2 = P.use_bank(5) if (tt == 0 and j == 0) else None
                w = []
                if j == 0:
                    w += [("se", kv_), ("ve", vv_)]
                if bwait:
                    w.append(bwait)
                if bwait2:
                    w.append(bwait2)
                P.op("tensor", lambda c, b=b, j=j, tt=tt:
                     c["eng"].matmul(bank(4)[:, j * 128:j * 128 + 128],
                                     ktile[b][:, j * 128:j * 128 + 128],
                                     vtile[b][:, j * 128:j * 128 + 128],
                                     start=(tt == 0 and j == 0), stop=(tt == NTT - 1),
                                     skip_group_check=True),
                     waits=w, inc=None)
                P.op("tensor", lambda c, b=b, j=j, tt=tt:
                     c["eng"].matmul(bank(5)[:, j:j + 1],
                                     ktile[b][:, j * 128:j * 128 + 128], ones_col[:],
                                     start=(tt == 0 and j == 0), stop=(tt == NTT - 1),
                                     skip_group_check=True),
                     inc="te" if j == 3 else None)
            tstate = P.val("te")
            ktile_free[b] = ("te", tstate)
            vtile_free[b] = ("te", tstate)
        t = P.val("te")
        v = P.op("vector", lambda c, half=half: c["eng"].tensor_copy(
            ct_f32[:, half * 512:half * 512 + 512], bank(4)),
            waits=[("te", t)], inc="ve")
        P.release_bank(4, "ve", v)
        v2 = P.op("vector", lambda c, half=half: c["eng"].tensor_copy(
            ct_f32[:, 1024 + half * 4:1024 + half * 4 + 4], bank(5)[:, 0:4]),
            waits=[("te", t)], inc="ve")
        P.release_bank(5, "ve", v2)
        ct_evac.append(v2)
        if half == 0:
            load_wkv(1, [("te", t)])

    ar_st = P.op("sync", lambda c: c["eng"].dma_start(
        out=AR_IN, in_=ct_f32[:]),
        waits=[("ve", ct_evac[-1])], inc="dma10", inc_n=16)
    if debug == "B1":
        P.op("gpsimd", lambda c: c["eng"].dma_start(out=dbg["ctp"], in_=AR_IN),
             waits=[("dma10", ar_st)], inc="gst", inc_n=16)
    cc_v = P.op("gpsimd", lambda c: c["eng"].collective_compute(
        "AllReduce", ALU.add, replica_groups=[[0, 1], [2, 3], [4, 5], [6, 7]],
        ins=[AR_IN], outs=[AR_OUT]),
        waits=[("dma10", ar_st)], inc="cc")
    ld_ar = P.op("sync", lambda c: c["eng"].dma_start(
        out=ct_f32[:], in_=AR_OUT),
        waits=[("cc", cc_v)], inc="dma11", inc_n=16)
    mzc = P.op("vector", lambda c: c["eng"].memset(ct_bf[:], 0.0),
               waits=list(ph0_fence), inc="ve")
    mzk = P.op("vector", lambda c: c["eng"].memset(kdiag[:], 0.0),
               waits=list(ph0_fence), inc="ve")
    for hp in range(8):
        w = [("dma11", ld_ar)]
        if hp == 0:
            w.append(("ve", mzk))
        P.op("vector", lambda c, hp=hp: c["eng"].tensor_copy(
            ct_bf[0:64, hp, 0:64], ct_f32[0:64, hp * 128:hp * 128 + 64]),
            waits=w, inc="ve")
        P.op("vector", lambda c, hp=hp: c["eng"].tensor_copy(
            ct_bf[64:128, hp, 64:128], ct_f32[64:128, hp * 128 + 64:hp * 128 + 128]),
            inc="ve")
        P.op("vector", lambda c, hp=hp: c["eng"].tensor_copy(
            kdiag[0:64, hp, 2 * hp:2 * hp + 1], ct_f32[0:64, 1024 + hp:1024 + hp + 1]),
            inc="ve")
        P.op("vector", lambda c, hp=hp: c["eng"].tensor_copy(
            kdiag[64:128, hp, 2 * hp + 1:2 * hp + 2],
            ct_f32[64:128, 1024 + hp:1024 + hp + 1]),
            inc="ve")
    cb = P.val("ve")
    kd = P.val("ve")
    z2_done = {}
    for tg in range(4):
        bk = tg
        bw = P.use_bank(bk)
        for dc in range(8):
            P.op("tensor", lambda c, dc=dc, tg=tg, bk=bk: c["eng"].matmul(
                bank(bk)[0:16, :], kdiag[:, dc, :], q_fm[:, dc, tg * 512:tg * 512 + 512],
                start=(dc == 0), stop=(dc == 7)),
                waits=([("ve", kd)] + ([bw] if bw else [])) if dc == 0 else (),
                inc="te" if dc == 7 else None)
        t = P.val("te")
        v1 = P.op("vector", lambda c, tg=tg, bk=bk: c["eng"].tensor_scalar_add(
            ct_f32[0:16, 0:512], bank(bk)[0:16, :], Z_EPS),
            waits=[("te", t)], inc="ve")
        P.release_bank(bk, "ve", v1)
        z2_done[tg] = P.op("vector", lambda c, tg=tg: c["eng"].reciprocal(
            z_sb[:, tg * 512:tg * 512 + 512], ct_f32[0:16, 0:512]),
            waits=[("ve", v1)], inc="ve")

    if debug == "B1":
        P.op("sync", lambda c: c["eng"].dma_start(out=dbg["q"], in_=q_fm[:]),
             waits=[("se", P.val("se"))], inc="gst", inc_n=16)
        P.op("sync", lambda c: c["eng"].dma_start(out=dbg["ct"], in_=ct_f32[:]),
             waits=[("dma11", ld_ar)], inc="gst", inc_n=16)
        P.op("sync", lambda c: c["eng"].dma_start(out=dbg["z"], in_=z_sb[:]),
             waits=[("ve", z2_done[3])], inc="gst", inc_n=16)
        dbg_waits = [("gst", P.val("gst"))]

    attn = xln
    zbc_free = {0: None}
    attn_last = None
    for hp in range(8):
        for tg in range(4):
            bo = 4 + (tg % 2)
            bz = 6 + (tg % 2)
            bwz = P.use_bank(bz)
            bwo = P.use_bank(bo)
            P.op("tensor", lambda c, hp=hp, tg=tg, bz=bz: c["eng"].matmul(
                bank(bz), ssel_sb[:, hp, :], z_sb[:, tg * 512:tg * 512 + 512],
                start=True, stop=True),
                waits=[("ve", z2_done[tg]), ssel_loaded]
                + ([bwz] if bwz else []) + dbg_waits,
                inc="te")
            tz = P.val("te")
            P.op("tensor", lambda c, hp=hp, tg=tg, bo=bo: c["eng"].matmul(
                bank(bo), ct_bf[:, hp, :],
                q_fm[:, hp, tg * 512:tg * 512 + 512],
                start=True, stop=True),
                waits=[("ve", cb)] + ([bwo] if bwo else []), inc="te")
            to = P.val("te")
            wz = [("te", tz)] + ([zbc_free[0]] if zbc_free[0] else [])
            sz = P.op("scalar", lambda c, tg=tg, bz=bz: c["eng"].activation(
                zbc_sb[0][:], bank(bz), AF.Identity), waits=wz, inc="se")
            P.release_bank(bz, "se", sz)
            av = P.op("vector", lambda c, hp=hp, tg=tg, bo=bo: c["eng"].tensor_tensor(
                attn[:, hp, tg * 512:tg * 512 + 512], bank(bo), zbc_sb[0][:],
                ALU.mult), waits=[("te", to), ("se", sz)], inc="ve")
            P.release_bank(bo, "ve", av)
            zbc_free[0] = ("ve", av)
            attn_last = av

    if debug == "B1":
        P.op("sync", lambda c: c["eng"].dma_start(out=dbg["attn"], in_=attn[:]),
             waits=[("ve", attn_last)], inc="gst", inc_n=16)

    def load_wproj(i, waits):
        tg, oc = divmod(i, 8)
        load_wstream(("p", i), WPROJT[:, oc * 128:oc * 128 + 128].rearrange(
            "(dc p) o -> p dc o", p=128), waits, i % 2)

    load_wproj(0, [])
    load_wproj(1, [])
    stage_free = None
    res_done = []
    for tg in range(4):
        stage_evacs = []
        for oc in range(8):
            i = tg * 8 + oc
            bk = oc % 4
            bw = P.use_bank(bk)
            w0 = [("ve", attn_last), wq_tgt[("p", i)]]
            if bw:
                w0.append(bw)
            if stage_free is not None:
                w0.append(stage_free)
            for dc in range(8):
                P.op("tensor", lambda c, oc=oc, dc=dc, tg=tg, bk=bk, i=i: c["eng"].matmul(
                    bank(bk), wstr[i % 2][:, dc, :],
                    attn[:, dc, tg * 512:tg * 512 + 512],
                    start=(dc == 0), stop=(dc == 7)),
                    waits=w0 if dc == 0 else (), inc="te" if dc == 7 else None)
            t = P.val("te")
            wstr_free[i % 2] = ("te", t)
            if i + 2 < 32:
                load_wproj(i + 2, [])
            sv = P.op("scalar", lambda c, oc=oc, bk=bk: c["eng"].activation(
                stage[:, oc, :], bank(bk), AF.Identity, bias=bias_all[:, oc:oc + 1]),
                waits=[("te", t)], inc="se")
            P.release_bank(bk, "se", sv)
            stage_evacs.append(sv)
        for q in range(4):
            tt = tg * 4 + q
            rd = emit_epilogue_tile(
                tt, (lambda oc, q=q: stage[:, oc, q * 128:q * 128 + 128]),
                [("se", stage_evacs[-1])])
            res_done.append(rd)
        stage_free = ("te", P.val("te"))
    ph1_fence = P.fence()
    ph1.close()
    stats_state["sum_ready"] = res_done[-1]

    def store_res_and_finish(close_stack=None):
        for g in range(4):
            P.op("sync", lambda c, g=g: c["eng"].dma_start(
                out=OUT.rearrange("(t p) d -> p t d", p=128)[:, 4 * g:4 * g + 4, :],
                in_=res[:, 4 * g:4 * g + 4, :]),
                waits=[res_done[4 * g + 3]], inc="gst", inc_n=16)
        fin = P.val("gst")
        P.op("gpsimd", lambda c, fin=fin: c["eng"].engine_nop(), waits=[("gst", fin)])
        if close_stack is not None:
            close_stack.close()
        return finish(P, sem, block, top, nc)

    if debug == "B1":
        return store_res_and_finish()

    if debug == "B1X":
        W1Bx = nc.dram_tensor("w1bx", [D, 4 * D], BF16, kind="Internal").ap()
        W2Bx = nc.dram_tensor("w2bx", [4 * D, D], BF16, kind="Internal").ap()
        P.op("gpsimd", lambda c: c["eng"].dma_start(out=W1Bx, in_=W1), inc="dma2", inc_n=16)
        P.op("gpsimd", lambda c: c["eng"].dma_start(out=W2Bx, in_=W2), inc="dma2", inc_n=16)
        return store_res_and_finish()

    if debug == "B1Y":
        emit_ln_to_xln(fence_waits=list(P.fence()))
        return store_res_and_finish()

    ph4 = ExitStack()
    e4 = ph4.enter_context
    h_grp = e4(nc.sbuf_tensor("h_grp", [128, 32, D], BF16))
    w1s = [e4(nc.sbuf_tensor(f"w1s{i}", [128, NDC, 128], BF16)) for i in range(2)]
    w2s = [e4(nc.sbuf_tensor(f"w2s{i}", [128, D], BF16)) for i in range(2)]

    W1B = nc.dram_tensor("w1b", [D, 4 * D], BF16, kind="Internal").ap()
    W2B = nc.dram_tensor("w2b", [4 * D, D], BF16, kind="Internal").ap()
    P.op("gpsimd", lambda c: c["eng"].dma_start(out=W1B, in_=W1), inc="dma2", inc_n=16)
    P.op("gpsimd", lambda c: c["eng"].dma_start(out=W2B, in_=W2), inc="dma2", inc_n=16)
    wb_done = ("dma2", P.val("dma2"))

    ph4_fence = P.fence()
    emit_ln_to_xln(fence_waits=list(ph4_fence))
    xln_ready4 = [("se", P.val("se")), ("ve", P.val("ve"))]
    if debug in ("B2", "B2LN"):
        P.op("sync", lambda c: c["eng"].dma_start(out=dbg["xln2"], in_=xln[:]),
             waits=xln_ready4, inc="gst", inc_n=16)
        xln_ready4 = xln_ready4 + [("gst", P.val("gst"))]
    if debug == "B2LN":
        ph4.close()
        return store_res_and_finish()

    res_done = []
    w1_tgt = {}
    w1_free = {0: None, 1: None}
    w2_tgt = {}
    w2_free = {0: None, 1: None}
    stage_free4 = None
    for pair in range(2):
        h_evac = []
        for ic in range(32):
            b = ic % 2
            w = [wb_done]
            if pair == 0 and ic == 0:
                w += list(ph4_fence)
            if w1_free[b] is not None:
                w.append(w1_free[b])
            sm1 = "dma7" if b == 0 else "dma8"
            P.op("sync", lambda c, ic=ic, b=b: c["eng"].dma_start(
                out=w1s[b][:], in_=W1B[:, ic * 128:ic * 128 + 128].rearrange(
                    "(dc p) o -> p dc o", p=128)), waits=w, inc=sm1, inc_n=16)
            w1_tgt[ic] = (sm1, P.val(sm1))
            for dc in range(NDC):
                for tgl in range(2):
                    bk = b * 2 + tgl
                    bw = P.use_bank(bk) if dc == 0 else None
                    tg = 2 * pair + tgl
                    P.op("tensor", lambda c, ic=ic, dc=dc, tg=tg, bk=bk, b=b: c["eng"].matmul(
                        bank(bk), w1s[b][:, dc, :], xln[:, dc, tg * 512:tg * 512 + 512],
                        start=(dc == 0), stop=(dc == NDC - 1)),
                        waits=([w1_tgt[ic]] + xln_ready4 + ([bw] if bw else []))
                        if dc == 0 else (), inc="te" if dc == NDC - 1 else None)
            t = P.val("te")
            w1_free[b] = ("te", t)
            for tgl in range(2):
                bk = b * 2 + tgl
                tstop = t - (1 - tgl)
                sv = P.op("scalar", lambda c, ic=ic, tgl=tgl, bk=bk: c["eng"].activation(
                    h_grp[:, ic, tgl * 512:tgl * 512 + 512], bank(bk), AF.Relu,
                    bias=bias_all[:, 8 + ic:8 + ic + 1]),
                    waits=[("te", tstop)], inc="se")
                P.release_bank(bk, "se", sv)
                h_evac.append(sv)
        if debug == "B2" and pair == 0:
            P.op("sync", lambda c: c["eng"].dma_start(out=dbg["h"], in_=h_grp[:]),
                 waits=[("se", h_evac[-1])], inc="gst", inc_n=16)
        for tgl in range(2):
            tg = 2 * pair + tgl
            y_te = {}
            for ic in range(32):
                b = ic % 2
                w = [wb_done]
                if w2_free[b] is not None:
                    w.append(w2_free[b])
                sm2 = "dma5" if b == 0 else "dma6"
                P.op("sync", lambda c, ic=ic, b=b: c["eng"].dma_start(
                    out=w2s[b][:], in_=W2B[ic * 128:ic * 128 + 128, :]),
                    waits=w, inc=sm2, inc_n=16)
                w2_tgt[ic] = (sm2, P.val(sm2))
                for oc in range(NDC):
                    bk = oc % 8
                    wts = []
                    if oc == 0:
                        wts.append(w2_tgt[ic])
                    if ic == 0:
                        bw = P.use_bank(bk)
                        if bw:
                            wts.append(bw)
                        if oc == 0:
                            wts.append(("se", h_evac[-1]))
                            if stage_free4 is not None:
                                wts.append(stage_free4)
                    P.op("tensor", lambda c, ic=ic, oc=oc, tgl=tgl, bk=bk, b=b: c["eng"].matmul(
                        bank(bk), w2s[b][:, oc * 128:oc * 128 + 128],
                        h_grp[:, ic, tgl * 512:tgl * 512 + 512],
                        start=(ic == 0), stop=(ic == 31)),
                        waits=wts, inc="te" if oc == NDC - 1 else None)
                w2_free[b] = ("te", P.val("te"))
            tlast = P.val("te")
            for oc in range(NDC):
                sv = P.op("scalar", lambda c, oc=oc: c["eng"].activation(
                    stage[:, oc, :], bank(oc % 8), AF.Identity,
                    bias=bias_all[:, 40 + oc:40 + oc + 1]),
                    waits=[("te", tlast)], inc="se")
                P.release_bank(oc % 8, "se", sv)
                stage_evac_last = sv
            for q in range(4):
                tt = tg * 4 + q
                rd = emit_epilogue_tile(
                    tt, (lambda oc, q=q: stage[:, oc, q * 128:q * 128 + 128]),
                    [("se", stage_evac_last)])
                res_done.append(rd)
            stage_free4 = ("te", P.val("te"))
    ph4.close()
    stats_state["sum_ready"] = res_done[-1]

    if debug == "B2":
        return store_res_and_finish()

    ph5 = ExitStack()
    e5 = ph5.enter_context
    q1 = e5(nc.sbuf_tensor("q1", [128, 4096], BF16))
    k1 = e5(nc.sbuf_tensor("k1", [128, 4096], BF16))
    v1 = e5(nc.sbuf_tensor("v1", [128, 32, 128], BF16))
    t_ext = [e5(nc.sbuf_tensor(f"text{i}", [33, 4096], BF16)) for i in range(3)]
    p_sb = [e5(nc.sbuf_tensor(f"psb{i}", [128, 1024], BF16)) for i in range(2)]
    av_st = [e5(nc.sbuf_tensor(f"av_st{i}", [128, 1024], BF16)) for i in range(2)]
    vqt_bf = e5(nc.sbuf_tensor("vqt_bf", [128, 3, 32], BF16))
    ue_bf = e5(nc.sbuf_tensor("ue_bf", [33, 3, 128], BF16))
    iden_bf = e5(nc.sbuf_tensor("iden_bf", [128, 128], BF16))

    av_free = {}
    ph5_fence = P.fence()
    emit_ln_to_xln(fence_waits=list(ph5_fence))
    xln_ready5 = [("se", P.val("se")), ("ve", P.val("ve"))]

    for i, src in enumerate((VQT, VKT, VVT)):
        P.op("gpsimd", lambda c, i=i, src=src: c["eng"].dma_start(
            out=vqt_bf[:, i, :], in_=src), waits=list(ph5_fence), inc="dma9", inc_n=16)
    for i, src in enumerate((UEQ, UEK, UEV)):
        P.op("gpsimd", lambda c, i=i, src=src: c["eng"].dma_start(
            out=ue_bf[0:33, i, :], in_=src), inc="dma9", inc_n=16)
    lr_loaded = ("dma9", P.val("dma9"))
    ib = P.op("vector", lambda c: c["eng"].tensor_copy(iden_bf[:], iden[:]),
              waits=list(ph5_fence), inc="ve")
    for i in range(3):
        P.op("vector", lambda c, i=i: c["eng"].memset(t_ext[i][32:33, :], 1.0),
             waits=list(ph5_fence), inc="ve")
    ones_rows = P.val("ve")

    dmp1 = P.op("sync", lambda c: c["eng"].dma_start(out=CC1_IN, in_=xln[:]),
                waits=xln_ready5, inc="dma10", inc_n=16)
    ag1 = P.op("gpsimd", lambda c: c["eng"].collective_compute(
        "AllGather", ALU.bypass, replica_groups=[[0, 1, 2, 3, 4, 5, 6, 7]],
        ins=[CC1_IN], outs=[CC1_OUT]), waits=[("dma10", dmp1)], inc="cc")
    xh = xln
    import dataclasses as _dc
    for r in range(8):
        base = CC1_OUT[r, :, 0, :]
        P.op("gpsimd", lambda c, r=r, base=base: c["eng"].dma_start(
            out=xh[:, r, :],
            in_=_dc.replace(base, offset=base.offset + c["pid"] * T_CORE)),
            waits=[("cc", ag1)] if r == 0 else (), inc="dma11", inc_n=16)
    xh_loaded = ("dma11", P.val("dma11"))

    for s4 in range(4):
        t_done = {}
        for pi in range(3):
            for n8 in range(8):
                bk = 6 + (n8 % 2)
                bw = P.use_bank(bk)
                w = [xh_loaded, lr_loaded]
                if bw:
                    w.append(bw)
                P.op("tensor", lambda c, pi=pi, n8=n8, s4=s4, bk=bk: c["eng"].matmul(
                    bank(bk)[0:32, :], vqt_bf[:, pi, :],
                    xh[:, 2 * s4 + n8 // 4, (n8 % 4) * 512:(n8 % 4) * 512 + 512],
                    start=True, stop=True),
                    waits=w, inc="te")
                t = P.val("te")
                w2_ = [("te", t)]
                tv = P.op("vector", lambda c, pi=pi, n8=n8, bk=bk: c["eng"].tensor_copy(
                    t_ext[pi][0:32, n8 * 512:n8 * 512 + 512], bank(bk)[0:32, :]),
                    waits=w2_ + [("ve", ones_rows)], inc="ve")
                P.release_bank(bk, "ve", tv)
        t_ready = P.val("ve")
        for pi, dst in ((0, q1), (1, k1)):
            for n8 in range(8):
                bk = 6 + (n8 % 2)
                bw = P.use_bank(bk)
                P.op("tensor", lambda c, pi=pi, n8=n8, bk=bk: c["eng"].matmul(
                    bank(bk), ue_bf[:, pi, :], t_ext[pi][:, n8 * 512:n8 * 512 + 512],
                    start=True, stop=True),
                    waits=[("ve", t_ready)] + ([bw] if bw else []), inc="te")
                t = P.val("te")
                sv = P.op("scalar", lambda c, dst=dst, n8=n8, bk=bk: c["eng"].activation(
                    dst[:, n8 * 512:n8 * 512 + 512], bank(bk), AF.Identity),
                    waits=[("te", t)], inc="se")
                P.release_bank(bk, "se", sv)
        for g8 in range(8):
            bk = 6 + (g8 % 2)
            bw = P.use_bank(bk)
            for q4 in range(4):
                ttk = g8 * 4 + q4
                P.op("tensor", lambda c, ttk=ttk, q4=q4, bk=bk: c["eng"].matmul(
                    bank(bk)[:, q4 * 128:q4 * 128 + 128],
                    t_ext[2][:, ttk * 128:ttk * 128 + 128], ue_bf[:, 2, :],
                    start=True, stop=True),
                    waits=([("ve", t_ready)] + ([bw] if bw else [])) if q4 == 0 else (),
                    inc="te" if q4 == 3 else None)
            t = P.val("te")
            sv = P.op("vector", lambda c, g8=g8, bk=bk: c["eng"].tensor_copy(
                v1[:, 4 * g8:4 * g8 + 4, :], bank(bk).rearrange("p (q n) -> p q n", q=4)),
                waits=[("te", t)], inc="ve")
            P.release_bank(bk, "ve", sv)
        qkv_ready = [("se", P.val("se")), ("ve", P.val("ve"))]

        for qcp in range(4):
            bw4 = P.use_bank(4)
            bw5 = P.use_bank(5)
            psb_free = {0: None, 1: None}
            for kt in range(32):
                bsc = kt % 2
                bwsc1 = P.use_bank(2 * bsc)
                bwsc2 = P.use_bank(2 * bsc + 1)
                w = list(qkv_ready) + [x for x in (bwsc1, bwsc2) if x]
                for h2 in range(2):
                    P.op("tensor", lambda c, kt=kt, qcp=qcp, h2=h2, bsc=bsc: c["eng"].matmul(
                        pq[bsc][:, h2 * 512:h2 * 512 + 512],
                        k1[:, kt * 128:kt * 128 + 128],
                        q1[:, qcp * 1024 + h2 * 512:qcp * 1024 + h2 * 512 + 512],
                        start=True, stop=True),
                        waits=w if h2 == 0 else (), inc="te" if h2 == 1 else None)
                tsc = P.val("te")
                we = [("te", tsc)]
                if psb_free[kt % 2] is not None:
                    we.append(psb_free[kt % 2])
                ex = P.op("scalar", lambda c, kt=kt: c["eng"].activation(
                    p_sb[kt % 2][:], pq[kt % 2][:], AF.Exp, scale=INV_SQRT_DH),
                    waits=we, inc="se")
                P.release_bank(2 * bsc, "se", ex)
                P.release_bank(2 * bsc + 1, "se", ex)
                wav = [("se", ex)]
                if kt == 0:
                    wav += [x for x in (bw4, bw5) if x]
                for h2 in range(2):
                    P.op("tensor", lambda c, kt=kt, h2=h2: c["eng"].matmul(
                        bank(4 + h2), v1[:, kt, :],
                        p_sb[kt % 2][:, h2 * 512:h2 * 512 + 512],
                        start=(kt == 0), stop=(kt == 31)),
                        waits=wav if h2 == 0 else (), inc="te" if h2 == 1 else None)
                psb_free[kt % 2] = ("te", P.val("te"))
            tav = P.val("te")
            pb5 = qcp % 2
            avw = [("te", tav)]
            if av_free.get(pb5):
                avw.append(av_free[pb5])
            for h2 in range(2):
                sv = P.op("vector", lambda c, pb5=pb5, h2=h2: c["eng"].tensor_copy(
                    av_st[pb5][:, h2 * 512:h2 * 512 + 512],
                    bank(4 + h2)), waits=avw if h2 == 0 else [("te", tav)], inc="ve")
                P.release_bank(4 + h2, "ve", sv)
            smav = "dma5" if pb5 == 0 else "dma6"
            stv = P.op("sync", lambda c, s4=s4, qcp=qcp, pb5=pb5: c["eng"].dma_start(
                out=CC2_IN[:, s4, qcp * 1024:qcp * 1024 + 1024], in_=av_st[pb5][:]),
                waits=[("ve", P.val("ve"))], inc=smav, inc_n=16)
            av_free[pb5] = (smav, stv)

    dmp2 = P.val("dma5")
    ag2w = [("dma5", P.val("dma5")), ("dma6", P.val("dma6"))]
    ag2 = P.op("gpsimd", lambda c: c["eng"].collective_compute(
        "AllGather", ALU.bypass, replica_groups=[[0, 1, 2, 3, 4, 5, 6, 7]],
        ins=[CC2_IN], outs=[CC2_OUT]), waits=ag2w, inc="cc")
    attn5 = xln
    for r in range(8):
        base = CC2_OUT[r, :, :]
        P.op("gpsimd", lambda c, r=r, base=base: c["eng"].dma_start(
            out=attn5[:, r, :],
            in_=_dc.replace(base[:, 0:T_CORE],
                            offset=base.offset + c["pid"] * T_CORE)),
            waits=[("cc", ag2)] if r == 0 else (), inc="dma11", inc_n=16)
    attn5_loaded = ("dma11", P.val("dma11"))

    res_done = []
    stage_free5 = None
    for tg in range(4):
        cps = []
        for oc in range(NDC):
            w = [attn5_loaded]
            if stage_free5 is not None and oc == 0:
                w.append(stage_free5)
            cp = P.op("vector", lambda c, oc=oc, tg=tg: c["eng"].tensor_copy(
                stage[:, oc, :], attn5[:, oc, tg * 512:tg * 512 + 512]),
                waits=w, inc="ve")
            cps.append(cp)
        for q in range(4):
            tt = tg * 4 + q
            rd = emit_epilogue_tile(
                tt, (lambda oc, q=q: stage[:, oc, q * 128:q * 128 + 128]),
                [("ve", cps[-1])])
            res_done.append(rd)
        stage_free5 = ("te", P.val("te"))
    ph5.close()
    stats_state["sum_ready"] = res_done[-1]

    if debug == "B3":
        return store_res_and_finish()

    ph6 = ExitStack()
    e6 = ph6.enter_context
    m_sb = e6(nc.sbuf_tensor("m_sb", [128, NDC, D], BF16))

    ph6_fence = P.fence()
    emit_ln_to_xln(fence_waits=list(ph6_fence))
    xln_ready6 = [("se", P.val("se")), ("ve", P.val("ve"))]
    P.op("sync", lambda c: c["eng"].dma_start(out=m_sb[:], in_=M_DRAM),
         waits=m_stored + list(ph6_fence), inc="dma1", inc_n=16)
    m_loaded = ("dma1", P.val("dma1"))

    res_done = []
    stage_free6 = None
    for tg in range(4):
        for oc in range(NDC):
            bk = oc % 4
            bw = P.use_bank(bk)
            w0 = [m_loaded] + xln_ready6
            if bw:
                w0.append(bw)
            if stage_free6 is not None:
                w0.append(stage_free6)
            for jc in range(NDC):
                P.op("tensor", lambda c, oc=oc, jc=jc, tg=tg, bk=bk: c["eng"].matmul(
                    bank(bk), m_sb[:, jc, oc * 128:oc * 128 + 128],
                    xln[:, jc, tg * 512:tg * 512 + 512],
                    start=(jc == 0), stop=(jc == NDC - 1)),
                    waits=w0 if jc == 0 else (), inc="te" if jc == NDC - 1 else None)
            t = P.val("te")
            sv = P.op("scalar", lambda c, oc=oc, bk=bk: c["eng"].activation(
                stage[:, oc, :], bank(bk), AF.Identity,
                bias=bias_all[:, 48 + oc:48 + oc + 1]),
                waits=[("te", t)], inc="se")
            P.release_bank(bk, "se", sv)
            stage_evac_last6 = sv
        for q in range(4):
            tt = tg * 4 + q
            rd = emit_epilogue_tile(
                tt, (lambda oc, q=q: stage[:, oc, q * 128:q * 128 + 128]),
                [("se", stage_evac_last6)])
            res_done.append(rd)
        stage_free6 = ("te", P.val("te"))
    return store_res_and_finish(ph6)


def _host_inputs(inputs):
    C = host_constants()
    x = np.asarray(inputs["L0_attn_x"], dtype=np.float32).reshape(16384, D)
    wqkvT = np.ascontiguousarray(np.asarray(inputs["L0_attn_w_qkv"], np.float32).T)
    wprojT = np.ascontiguousarray(np.asarray(inputs["L0_attn_w_proj"], np.float32).T)
    w1 = np.ascontiguousarray(np.asarray(inputs["L0_ffn_W1"], np.float32))
    w2 = np.ascontiguousarray(np.asarray(inputs["L0_ffn_W2"], np.float32))
    wfTr = np.ascontiguousarray(np.asarray(inputs["L1_ffn_w_freq_real"], np.float32).T)
    wfTi = np.ascontiguousarray(np.asarray(inputs["L1_ffn_w_freq_imag"], np.float32).T)
    bproj = np.ascontiguousarray(np.asarray(inputs["L0_attn_b_proj"], np.float32).reshape(8, 128).T)
    b1 = np.ascontiguousarray(np.asarray(inputs["L0_ffn_b1"], np.float32).reshape(32, 128).T)
    b2 = np.ascontiguousarray(np.asarray(inputs["L0_ffn_b2"], np.float32).reshape(8, 128).T)
    l1b = np.ascontiguousarray(np.asarray(inputs["L1_ffn_bias"], np.float32).reshape(8, 128).T)
    ssel = np.zeros((16, 8, 128), np.float32)
    for hp in range(8):
        ssel[2 * hp, hp, 0:64] = 1.0
        ssel[2 * hp + 1, hp, 64:128] = 1.0

    Uq = np.asarray(inputs["L1_attn_Uq"], np.float32)
    Vq = np.asarray(inputs["L1_attn_Vq"], np.float32)
    bq = np.asarray(inputs["L1_attn_bq"], np.float32)
    Uk = np.asarray(inputs["L1_attn_Uk"], np.float32)
    Vk = np.asarray(inputs["L1_attn_Vk"], np.float32)
    bk = np.asarray(inputs["L1_attn_bk"], np.float32)
    Uv = np.asarray(inputs["L1_attn_Uv"], np.float32)
    Vv = np.asarray(inputs["L1_attn_Vv"], np.float32)
    bv = np.asarray(inputs["L1_attn_bv"], np.float32)

    maps = []
    for c in range(8):
        h = c
        maps.append({
            "x": np.ascontiguousarray(x[c * T_CORE:(c + 1) * T_CORE]),
            "wqkvT": wqkvT, "wprojT": wprojT, "w1": w1, "w2": w2,
            "wfTr": wfTr, "wfTi": wfTi,
            "c_cos": C["COS"], "c_sin": C["SIN"], "c_gr": C["GR"],
            "c_gsin": C["GSIN"], "c_gineg": C["GINEG"],
            "iden": C["IDEN"],
            "bproj": bproj, "b1": b1, "b2": b2, "l1b": l1b,
            "vqT": np.ascontiguousarray(Vq[h].T), "vkT": np.ascontiguousarray(Vk[h].T),
            "vvT": np.ascontiguousarray(Vv[h].T),
            "ueq": np.ascontiguousarray(np.concatenate([Uq[h].T, bq[h][None, :]], 0)),
            "uek": np.ascontiguousarray(np.concatenate([Uk[h].T, bk[h][None, :]], 0)),
            "uev": np.ascontiguousarray(np.concatenate([Uv[h].T, bv[h][None, :]], 0)),
            "ssel": ssel,
        })
    return maps


def kernel(**inputs):
    maps = _host_inputs(inputs)
    nc = build_nc()
    res = run_bass_kernel_spmd(nc, maps, core_ids=list(range(8)))
    out = np.concatenate([np.asarray(r["y"]) for r in res.results], axis=0)
    return out.reshape(4, 4096, D)
